# version 01
# baseline (speedup 1.0000x reference)
import numpy as np
from contextlib import ExitStack
import concourse.bass as bass
import concourse.mybir as mybir
from concourse.bass_utils import run_bass_kernel_spmd

F32 = mybir.dt.float32
BF16 = mybir.dt.bfloat16
U32 = mybir.dt.uint32
AF = mybir.ActivationFunctionType
ALU = mybir.AluOpType
AXX = mybir.AxisListType.X

D = 1024
SEQ = 8192
NB = 4
NCORE = 8
TOWN = 4096
NFC = D // 128
EPS_LN = 1e-5


class Sem:
    def __init__(self, nc, name):
        self._cm = nc.semaphore(name)
        self.h = self._cm.__enter__()
        self.v = 0

    def inc(self, instr, n=1):
        instr.then_inc(self.h, n)
        self.v += n
        return instr

    def wait(self, eng, v=None):
        eng.wait_ge(self.h, self.v if v is None else v)


class Prog:
    def __init__(self):
        self.nc = bass.Bass("TRN2", target_bir_lowering=False)
        self.ctx = ExitStack()
        self.sems = {}

    def sem(self, name):
        if name not in self.sems:
            self.sems[name] = Sem(self.nc, name)
        return self.sems[name]

    def sbuf(self, name, shape, dt):
        return self.ctx.enter_context(self.nc.sbuf_tensor(name, shape, dt)).ap()

    def psum(self, name, shape=(128, 512), dt=F32):
        return self.ctx.enter_context(
            self.nc.psum_tensor(name, list(shape), dt)).ap()

    def dram_in(self, name, shape, dt=F32):
        return self.nc.dram_tensor(name, list(shape), dt, kind="ExternalInput").ap()

    def dram_out(self, name, shape, dt=F32):
        return self.nc.dram_tensor(name, list(shape), dt, kind="ExternalOutput").ap()

    def dram_tmp(self, name, shape, dt):
        return self.nc.dram_tensor(name, list(shape), dt, kind="Internal").ap()

    def close(self):
        self.ctx.close()


class Chain:

    def __init__(self, p, blk):
        self.p = p
        self.blk = blk
        self.tk = p.sem("tk")

    def v(self):
        return self.tk.v

    def grp(self, eng_name, fn, wait_v=None, dma=False, extra_waits=()):
        tk = self.tk
        wv = tk.v if wait_v is None else wait_v
        amt = 16 if dma else 1

        def wrap(eng):
            if wv > 0:
                eng.wait_ge(tk.h, wv)
            for s, v in extra_waits:
                eng.wait_ge(s.h, v)
            last = fn(eng)
            assert last is not None, "group fn must return its last instruction"
            last.then_inc(tk.h, amt)
        getattr(self.blk, eng_name)(wrap)
        tk.v += amt
        return tk.v

    def side(self, eng_name, fn, wait_v, sem, extra_waits=()):
        tk = self.tk
        holder = []

        def wrap(eng):
            if wait_v > 0:
                eng.wait_ge(tk.h, wait_v)
            for s, v in extra_waits:
                eng.wait_ge(s.h, v)
            instrs = fn(eng)
            assert isinstance(instrs, list) and instrs
            for i in instrs:
                i.then_inc(sem.h, 16)
            holder.append(len(instrs))
        getattr(self.blk, eng_name)(wrap)
        sem.v += 16 * holder[0]
        return sem.v




DT_SIZE = {F32: 4, BF16: 2, U32: 4}


class Arena:

    def __init__(self, p, name, nbytes):
        self.nb = (nbytes + 3) // 4 * 4
        self.t = p.sbuf(name, [128, self.nb // 4], F32)
        self.off = 0

    def alloc(self, rows, cols, dt):
        sz = DT_SIZE[dt]
        self.off = (self.off + 63) // 64 * 64
        o = self.off
        self.off += cols * sz
        assert self.off <= self.nb, f"arena overflow {self.off} > {self.nb}"
        if dt == F32:
            return self.t[0:rows, o // 4:o // 4 + cols]
        return self.t.bitcast(dt)[0:rows, o // sz:o // sz + cols]

    def reset(self):
        self.off = 0


def emit_consts(p, ch):
    c = {}
    c["ones_col"] = p.sbuf("c_ones_col", [128, 1], F32)
    c["ones_row"] = p.sbuf("c_ones_row", [1, 512], F32)
    c["ones_row_b"] = p.sbuf("c_ones_row_b", [1, 512], BF16)
    c["id_f"] = p.sbuf("c_id_f", [128, 128], F32)
    c["id_b"] = p.sbuf("c_id_b", [128, 128], BF16)

    def g_fn(g):
        g.memset(c["ones_col"][:], 1.0)
        g.memset(c["ones_row"][:], 1.0)
        g.memset(c["ones_row_b"][:], 1.0)
        return g.iota(c["id_f"][:], pattern=[[-1, 128]], base=0,
                      channel_multiplier=1,
                      allow_small_or_imprecise_dtypes=True)
    ch.grp("gpsimd", g_fn)

    ch.grp("vector", lambda v: v.tensor_scalar(
        out=c["id_f"][:], in0=c["id_f"][:], scalar1=0.0, scalar2=None,
        op0=ALU.is_equal))
    ch.grp("vector", lambda v: v.tensor_copy(out=c["id_b"][:], in_=c["id_f"][:]))
    return c


def alloc_ln(p):
    return {
        "t12": p.sbuf("ln_t12", [128, 12], F32),
        "mv": p.sbuf("ln_mv", [128, 2], F32),
        "rs": p.sbuf("ln_rs", [128, 1], F32),
    }


def emit_ln(ch, scr, x_f32, xln_out):
    def v1(v):
        v.bn_stats(out=scr["t12"][:, 0:6], in_=x_f32[:, 0:512])
        return v.bn_stats(out=scr["t12"][:, 6:12], in_=x_f32[:, 512:1024])
    ch.grp("vector", v1)
    ch.grp("vector", lambda v: v.bn_aggr(out=scr["mv"][:], in_=scr["t12"][:]))
    ch.grp("vector", lambda v: v.tensor_scalar(
        out=scr["rs"][:], in0=scr["mv"][:, 1:2], scalar1=EPS_LN, scalar2=None,
        op0=ALU.add))
    ch.grp("scalar", lambda s: s.activation(out=scr["rs"][:], in_=scr["rs"][:],
                                            func=AF.Ln))
    ch.grp("scalar", lambda s: s.activation(out=scr["rs"][:], in_=scr["rs"][:],
                                            func=AF.Exp, scale=-0.5))
    ch.grp("vector", lambda v: v.tensor_scalar(
        out=xln_out, in0=x_f32, scalar1=scr["mv"][:, 0:1], scalar2=scr["rs"][:],
        op0=ALU.subtract, op1=ALU.mult))


def emit_transpose_1024(ch, c, xln_b, pt_b, xT, tc):
    def t_fn(t):
        for f in range(NFC):
            ins = t.transpose(pt_b[:, f * 128:(f + 1) * 128],
                              xln_b[:, f * 128:(f + 1) * 128], c["id_b"][:])
        return ins
    ch.grp("tensor", t_fn)

    def v_fn(v):
        for f in range(NFC):
            ins = v.tensor_copy(out=xT[f][:, tc * 128:(tc + 1) * 128],
                                in_=pt_b[:, f * 128:(f + 1) * 128])
        return ins
    ch.grp("vector", v_fn)


L1F_I = 2816
L1F_GU = 5632
L1F_E = 8
L1F_TB = 512
L1F_NTB = TOWN // L1F_TB
L1F_NIC = L1F_I // 128


def build_l1ffn(debug=False):
    p = Prog()
    nc = p.nc
    x3 = p.dram_in("x3", [TOWN, D])
    gate_w = p.dram_in("gate_w", [D, L1F_E])
    base_wgu = p.dram_in("base_wgu", [D, L1F_GU])
    base_wd = p.dram_in("base_wd", [L1F_I, D])
    exp_wgu = p.dram_in("exp_wgu", [L1F_E, D, L1F_GU])
    exp_wd = p.dram_in("exp_wd", [L1F_E, L1F_I, D])
    out = p.dram_out("out", [TOWN, D])
    if debug:
        dbg_xT = p.dram_out("dbg_xT", [NFC * 128, TOWN], BF16)
        dbg_cw = p.dram_out("dbg_cw", [128, 8 * (TOWN // 128)])
        dbg_aT = p.dram_out("dbg_aT", [L1F_NIC * 128, L1F_TB], BF16)
        dbg_y = p.dram_out("dbg_y", [128, D])

    xsb = [p.sbuf(f"xsb{i}", [128, D], F32) for i in range(2)]
    xlnb = p.sbuf("xlnb", [128, D], BF16)
    xT = [p.sbuf(f"xT{i}", [128, TOWN], BF16) for i in range(NFC)]
    gwb = p.sbuf("gwb", [128, NFC * 8], BF16)
    cw = p.sbuf("cw", [128, 8 * (TOWN // 128)], F32)
    gE = p.sbuf("gE", [128, 8], F32)
    gEm = p.sbuf("gEm", [128, 8], F32)
    gZ = p.sbuf("gZ", [128, 8], F32)
    gm1 = p.sbuf("gm1", [128, 1], F32)
    grs = p.sbuf("grs", [128, 1], F32)
    gmask = p.sbuf("gmask", [128, 8], F32)
    wg_r = [p.sbuf(f"wg{i}", [128, NFC * 128], BF16) for i in range(2)]
    wu_r = [p.sbuf(f"wu{i}", [128, NFC * 128], BF16) for i in range(2)]
    wd_sb = [p.sbuf(f"wdc{i}", [128, D], BF16) for i in range(L1F_NIC)]
    aT = [p.sbuf(f"aT{i}", [128, L1F_TB], BF16) for i in range(L1F_NIC)]
    silu_t = p.sbuf("silu_t", [128, 512], F32)
    accs = [p.sbuf(f"acc{i}", [128, D], F32) for i in range(L1F_TB // 128)]
    x3r = [p.sbuf(f"x3r{i}", [128, D], F32) for i in range(L1F_TB // 128)]
    ln_scr = alloc_ln(p)

    pg = p.psum("pg")
    pu = p.psum("pu")
    py = p.psum("py")
    pt = p.psum("pt")
    pt_b = pt.bitcast(BF16)

    ld = p.sem("ld")
    wl = p.sem("wl")
    wdl = p.sem("wdl")
    xl = p.sem("xl")
    st = p.sem("st")

    with nc.Block() as blk:
        ch = Chain(p, blk)
        c = emit_consts(p, ch)

        ch.side("gpsimd", lambda g: [g.dma_start(
            out=gwb[:].rearrange("p (f e) -> p f e", f=NFC),
            in_=gate_w.rearrange("(f p) e -> p f e", p=128))], 0, wl)
        gw_ready = wl.v

        NTC = TOWN // 128
        buf_free_v = [0, 0]
        for tc in range(NTC):
            bi = tc % 2
            ch.side("sync", lambda q, tc=tc, bi=bi: [q.dma_start(
                out=xsb[bi][:], in_=x3[tc * 128:(tc + 1) * 128, :])],
                buf_free_v[bi], ld)
            need_ld = ld.v
            def v1(v, tc=tc, bi=bi):
                v.bn_stats(out=ln_scr["t12"][:, 0:6], in_=xsb[bi][:, 0:512])
                return v.bn_stats(out=ln_scr["t12"][:, 6:12],
                                  in_=xsb[bi][:, 512:1024])
            ch.grp("vector", v1, extra_waits=((ld, need_ld),))
            ch.grp("vector", lambda v: v.bn_aggr(out=ln_scr["mv"][:],
                                                 in_=ln_scr["t12"][:]))
            ch.grp("vector", lambda v: v.tensor_scalar(
                out=ln_scr["rs"][:], in0=ln_scr["mv"][:, 1:2], scalar1=EPS_LN,
                scalar2=None, op0=ALU.add))
            ch.grp("scalar", lambda s: s.activation(
                out=ln_scr["rs"][:], in_=ln_scr["rs"][:], func=AF.Ln))
            ch.grp("scalar", lambda s: s.activation(
                out=ln_scr["rs"][:], in_=ln_scr["rs"][:], func=AF.Exp,
                scale=-0.5))

            def v2(v, bi=bi):
                return v.tensor_scalar(out=xlnb[:], in0=xsb[bi][:],
                                       scalar1=ln_scr["mv"][:, 0:1],
                                       scalar2=ln_scr["rs"][:],
                                       op0=ALU.subtract, op1=ALU.mult)
            buf_free_v[bi] = ch.grp("vector", v2)

            emit_transpose_1024(ch, c, xlnb, pt_b, xT, tc)

            def t_g(t, tc=tc):
                for f in range(NFC):
                    ins = t.matmul(pg[:, 0:8], xT[f][:, tc * 128:(tc + 1) * 128],
                                   gwb[:, f * 8:(f + 1) * 8],
                                   start=(f == 0), stop=(f == NFC - 1))
                return ins
            ch.grp("tensor", t_g, extra_waits=((wl, gw_ready),))

            ch.grp("vector", lambda v: v.tensor_reduce(
                out=gm1[:], in_=pg[:, 0:8], axis=AXX, op=ALU.max))
            ch.grp("vector", lambda v: v.tensor_scalar(
                out=gm1[:], in0=gm1[:], scalar1=-1.0, scalar2=None, op0=ALU.mult))

            def s_e(s):
                return s.activation(out=gE[:], in_=pg[:, 0:8], func=AF.Exp,
                                    bias=gm1[:])
            ch.grp("scalar", s_e)

            ch.grp("vector", lambda v: v.max(out=gEm[:], in_=gE[:]))
            ch.grp("vector", lambda v: v.tensor_tensor(
                out=grs[:], in0=gEm[:, 0:1], in1=gEm[:, 1:2], op=ALU.add))
            ch.grp("vector", lambda v: v.reciprocal(out=grs[:], in_=grs[:]))
            ch.grp("vector", lambda v: v.tensor_copy(out=gZ[:], in_=gE[:]))
            ch.grp("vector", lambda v: v.memset(gEm[:, 2:8], -3.0e38))
            ch.grp("vector", lambda v: v.match_replace(
                out=gZ[:], in_to_replace=gEm[:], in_values=gZ[:], imm_value=-1.0))
            ch.grp("vector", lambda v: v.tensor_scalar(
                out=gmask[:], in0=gZ[:], scalar1=-1.0, scalar2=None,
                op0=ALU.is_equal))
            ch.grp("vector", lambda v: v.tensor_tensor(
                out=gmask[:], in0=gmask[:], in1=gE[:], op=ALU.mult))
            ch.grp("vector", lambda v, tc=tc: v.tensor_scalar(
                out=cw[:, tc * 8:(tc + 1) * 8], in0=gmask[:], scalar1=grs[:],
                scalar2=None, op0=ALU.mult))

        if debug:
            def g_dbg1(g):
                ins = [g.dma_start(out=dbg_xT[f * 128:(f + 1) * 128, :],
                                   in_=xT[f][:]) for f in range(NFC)]
                ins.append(g.dma_start(out=dbg_cw, in_=cw[:]))
                return ins
            ch.side("gpsimd", g_dbg1, ch.v(), p.sem("st"))

        NTCH = L1F_TB // 128
        wd_free_v = 0
        x3r_free_v = [0] * NTCH
        for tb in range(L1F_NTB):
            t0 = tb * L1F_TB
            x3_need = []
            for tch in range(NTCH):
                ch.side("sync", lambda q, tch=tch: [q.dma_start(
                    out=x3r[tch][:],
                    in_=x3[t0 + tch * 128:t0 + (tch + 1) * 128, :])],
                    x3r_free_v[tch], xl)
                x3_need.append(xl.v)
            for br in range(1 + L1F_E):
                wgu_src = base_wgu if br == 0 else exp_wgu[br - 1]
                wd_src = base_wd if br == 0 else exp_wd[br - 1]

                def g_wd(g, wd_src=wd_src):
                    return [g.dma_start(out=wd_sb[ic][:],
                                        in_=wd_src[ic * 128:(ic + 1) * 128, :])
                            for ic in range(L1F_NIC)]
                ch.side("gpsimd", g_wd, wd_free_v, wdl)
                wd_ready = wdl.v

                wg_need = [0, 0]
                wg_free_v = [0, 0]
                for ic in range(L1F_NIC):
                    ring = ic % 2

                    def g_wgu(g, ic=ic, ring=ring, wgu_src=wgu_src):
                        return [g.dma_start(
                            out=wg_r[ring][:].rearrange("p (f c) -> p f c", f=NFC),
                            in_=wgu_src[:, ic * 128:(ic + 1) * 128]
                                .rearrange("(f p) c -> p f c", p=128)),
                            g.dma_start(
                            out=wu_r[ring][:].rearrange("p (f c) -> p f c", f=NFC),
                            in_=wgu_src[:, L1F_I + ic * 128:L1F_I + (ic + 1) * 128]
                                .rearrange("(f p) c -> p f c", p=128))]
                    ch.side("gpsimd", g_wgu, wg_free_v[ring], wl)
                    wg_need[ring] = wl.v

                    def t_gu(t, ic=ic, ring=ring, tb=tb):
                        for f in range(NFC):
                            t.matmul(pg[:], wg_r[ring][:, f * 128:(f + 1) * 128],
                                     xT[f][:, t0:t0 + 512],
                                     start=(f == 0), stop=(f == NFC - 1))
                        for f in range(NFC):
                            ins = t.matmul(pu[:],
                                           wu_r[ring][:, f * 128:(f + 1) * 128],
                                           xT[f][:, t0:t0 + 512],
                                           start=(f == 0), stop=(f == NFC - 1))
                        return ins
                    wg_free_v[ring] = ch.grp("tensor", t_gu,
                                             extra_waits=((wl, wg_need[ring]),))

                    ch.grp("scalar", lambda s: s.activation(
                        out=silu_t[:], in_=pg[:], func=AF.Silu))

                    def v_a(v, ic=ic):
                        return v.tensor_tensor(out=aT[ic][:], in0=silu_t[:],
                                               in1=pu[:], op=ALU.mult)
                    ch.grp("vector", v_a)
                if debug and tb == 0 and br == 0:
                    def g_dbg2(g):
                        return [g.dma_start(out=dbg_aT[ic * 128:(ic + 1) * 128, :],
                                            in_=aT[ic][:]) for ic in range(L1F_NIC)]
                    ch.side("gpsimd", g_dbg2, ch.v(), p.sem("st"))
                for tch in range(NTCH):
                    for oc in range(2):
                        def t_y(t, tch=tch, oc=oc):
                            for ic in range(L1F_NIC):
                                ins = t.matmul(
                                    py[:],
                                    aT[ic][:, tch * 128:(tch + 1) * 128],
                                    wd_sb[ic][:, oc * 512:(oc + 1) * 512],
                                    start=(ic == 0), stop=(ic == L1F_NIC - 1))
                            return ins
                        yv = ch.grp("tensor", t_y,
                                    extra_waits=((wdl, wd_ready),))
                        if tch == NTCH - 1 and oc == 1:
                            wd_free_v = yv

                        def v_c(v, tb=tb, br=br, tch=tch, oc=oc):
                            tcg = tb * NTCH + tch
                            if br == 0:
                                return v.tensor_tensor(
                                    out=accs[tch][:, oc * 512:(oc + 1) * 512],
                                    in0=x3r[tch][:, oc * 512:(oc + 1) * 512],
                                    in1=py[:], op=ALU.add)
                            return v.scalar_tensor_tensor(
                                out=accs[tch][:, oc * 512:(oc + 1) * 512],
                                in0=py[:],
                                scalar=cw[:, tcg * 8 + (br - 1):tcg * 8 + br],
                                in1=accs[tch][:, oc * 512:(oc + 1) * 512],
                                op0=ALU.mult, op1=ALU.add)
                        if br == 0:
                            ew = ((xl, x3_need[tch]), (st, st.v))
                        else:
                            ew = ()
                        vcv = ch.grp("vector", v_c, extra_waits=ew)
                        if br == 0 and oc == 1:
                            x3r_free_v[tch] = vcv
                        if debug and tb == 0 and br == 0 and tch == 0 and oc == 1:
                            ch.side("sync", lambda q: [q.dma_start(
                                out=dbg_y, in_=accs[0][:])], ch.v(), p.sem("st"))
            for tch in range(NTCH):
                ch.side("sync", lambda q, tch=tch: [q.dma_start(
                    out=out[t0 + tch * 128:t0 + (tch + 1) * 128, :],
                    in_=accs[tch][:])],
                    ch.v(), st)
        ch.grp("sync", lambda q: q.nop(), extra_waits=((st, st.v),))
    p.close()
    return p




L1A_H = 16
L1A_DH = 64
L1A_M = 256
L1A_EPS = 1e-6
L1A_DN = 64.0 ** -0.25


def build_l1attn(ntc1=64, ntc2=32, stage=9, debug=False):
    p = Prog()
    nc = p.nc
    x2 = p.dram_in("x2", [SEQ, D])
    w_qkv = p.dram_in("w_qkv", [D, 3 * D])
    b_qkv = p.dram_in("b_qkv", [3 * D])
    w_out = p.dram_in("w_out", [D, D])
    b_out = p.dram_in("b_out", [D])
    proj = p.dram_in("proj", [L1A_H, L1A_M, L1A_DH])
    out = p.dram_out("out", [TOWN, D])
    qp_dram = p.dram_tmp("qp_scr", [TOWN, L1A_H * L1A_M], BF16)
    if debug:
        dbg_qp = p.dram_out("dbg_qp", [128, L1A_H * L1A_M], BF16)
        dbg_kp = p.dram_out("dbg_kp", [128, L1A_H * L1A_M], BF16)
        dbg_kv = p.dram_out("dbg_kv", [128, 2560])
        dbg_at = p.dram_out("dbg_at", [128, D], BF16)
        dbg_vc = p.dram_out("dbg_vc", [128, 16 * 65], BF16)

    xsb = [p.sbuf(f"xsb{i}", [128, D], F32) for i in range(2)]
    xlnb = p.sbuf("xlnb", [128, D], BF16)
    xlnT_c = p.sbuf("xlnT_c", [128, D], BF16)
    wqkv_sb = p.sbuf("wqkv_sb", [128, NFC * 3 * D // 128 * 128], BF16)
    bcolT = p.sbuf("bcolT", [128, 24], F32)
    brow_f = p.sbuf("brow_f", [1, 2048], F32)
    brow_b = p.sbuf("brow_b", [1, 3 * D], BF16)
    bout_b = p.sbuf("bout_b", [1, D], BF16)
    projT = p.sbuf("projT", [128, L1A_H * L1A_M], BF16)
    proj_tmp = p.sbuf("proj_tmp", [128, 2 * 64], BF16)
    qkT_c = p.sbuf("qkT_c", [128, 16 * 128], BF16)
    v_c = p.sbuf("v_c", [128, 16 * 65], BF16)
    xpe = p.sbuf("xpe", [128, L1A_H * L1A_M], F32)
    nm = p.sbuf("nm", [128, 16], F32)
    mx2 = p.sbuf("mx2", [128, 16], F32)
    kp_c = p.sbuf("kp_c", [128, L1A_H * L1A_M], BF16)
    kv_acc = p.sbuf("kv_acc", [128, 2560], F32)
    kvb = p.sbuf("kvb", [128, 2560], BF16)
    qp_r = [p.sbuf(f"qp_r{i}", [128, L1A_H * L1A_M], BF16) for i in range(2)]
    qpT = p.sbuf("qpT", [128, 32 * 128], BF16)
    dn = p.sbuf("dn", [128, 16], F32)
    attn_c = p.sbuf("attn_c", [128, D], BF16)
    attnT_c = p.sbuf("attnT_c", [128, D], BF16)
    wout_sb = p.sbuf("wout_sb", [128, NFC * D], BF16)
    x2r = [p.sbuf(f"x2r{i}", [128, D], F32) for i in range(2)]
    out_c = p.sbuf("out_c", [128, D], F32)
    ln_scr = alloc_ln(p)

    pb = [p.psum(f"pb{i}") for i in range(8)]
    pb_b = [t.bitcast(BF16) for t in pb]

    ld = p.sem("ld")
    wl = p.sem("wl")
    qs = p.sem("qs")
    ql = p.sem("ql")
    xl = p.sem("xl")
    st = p.sem("st")

    KO = 2048
    VO = 2048
    with nc.Block() as blk:
        ch = Chain(p, blk)
        c = emit_consts(p, ch)

        def g_w0(g):
            return [g.dma_start(
                out=wqkv_sb[:].rearrange("p (f c) -> p f c", f=NFC),
                in_=w_qkv.rearrange("(f p) c -> p f c", p=128)),
                g.dma_start(
                out=wout_sb[:].rearrange("p (f c) -> p f c", f=NFC),
                in_=w_out.rearrange("(f p) c -> p f c", p=128)),
                g.dma_start(out=brow_f[:], in_=b_qkv[0:2048].unsqueeze(0)),
                g.dma_start(out=brow_b[:], in_=b_qkv.unsqueeze(0)),
                g.dma_start(out=bout_b[:], in_=b_out.unsqueeze(0))]
        ch.side("gpsimd", g_w0, 0, wl)
        w_ready = wl.v

        def t_bc(t):
            for j in range(16):
                ins = t.transpose(pb[0][:, j:j + 1], brow_f[:, j * 128:(j + 1) * 128],
                                  c["id_f"][0:1, 0:1])
            return ins
        ch.grp("tensor", t_bc, extra_waits=((wl, w_ready),))
        ch.grp("vector", lambda v: v.tensor_copy(out=bcolT[:, 0:16],
                                                 in_=pb[0][:, 0:16]))

        for h in range(L1A_H if stage >= 1 else 0):
            ch.side("gpsimd", lambda g, h=h: [g.dma_start(
                out=proj_tmp[:].rearrange("p (a d) -> p a d", a=2),
                in_=proj[h].rearrange("(a p) d -> p a d", p=128))],
                ch.v(), wl)
            pj_need = wl.v

            def t_pj(t, h=h):
                t.transpose(pb_b[0][0:64, 0:128], proj_tmp[:, 0:64], c["id_b"][:])
                return t.transpose(pb_b[0][0:64, 128:256], proj_tmp[:, 64:128],
                                   c["id_b"][:])
            ch.grp("tensor", t_pj, extra_waits=((wl, pj_need),))
            ch.grp("vector", lambda v, h=h: v.tensor_copy(
                out=projT[0:64, h * 256:(h + 1) * 256],
                in_=pb_b[0][0:64, 0:256]))
            ch.grp("vector", lambda v, h=h: v.tensor_copy(
                out=projT[64:128, h * 256:(h + 1) * 256],
                in_=pb_b[0][0:64, 0:256]))
        if stage >= 2:
            ch.grp("vector", lambda v: v.memset(
                v_c[:].rearrange("p (h s) -> p h s", h=16)[:, :, 64:65], 1.0))

        NTC = ntc1 if stage >= 3 else 0
        buf_free = [0, 0]
        kv_zero = ch.grp("gpsimd", lambda g: g.memset(kv_acc[:], 0.0))
        for tc in range(NTC):
            bi = tc % 2
            ch.side("sync", lambda q, tc=tc, bi=bi: [q.dma_start(
                out=xsb[bi][:], in_=x2[tc * 128:(tc + 1) * 128, :])],
                buf_free[bi], ld)
            need_ld = ld.v
            ch.grp("vector", lambda v, bi=bi: v.bn_stats(
                out=ln_scr["t12"][:, 0:6], in_=xsb[bi][:, 0:512]),
                extra_waits=((ld, need_ld),))
            ch.grp("vector", lambda v, bi=bi: v.bn_stats(
                out=ln_scr["t12"][:, 6:12], in_=xsb[bi][:, 512:1024]))
            ch.grp("vector", lambda v: v.bn_aggr(out=ln_scr["mv"][:],
                                                 in_=ln_scr["t12"][:]))
            ch.grp("vector", lambda v: v.tensor_scalar(
                out=ln_scr["rs"][:], in0=ln_scr["mv"][:, 1:2], scalar1=EPS_LN,
                scalar2=None, op0=ALU.add))
            ch.grp("scalar", lambda s: s.activation(
                out=ln_scr["rs"][:], in_=ln_scr["rs"][:], func=AF.Ln))
            ch.grp("scalar", lambda s: s.activation(
                out=ln_scr["rs"][:], in_=ln_scr["rs"][:], func=AF.Exp, scale=-0.5))

            def v2(v, bi=bi):
                return v.tensor_scalar(out=xlnb[:], in0=xsb[bi][:],
                                       scalar1=ln_scr["mv"][:, 0:1],
                                       scalar2=ln_scr["rs"][:],
                                       op0=ALU.subtract, op1=ALU.mult)
            buf_free[bi] = ch.grp("vector", v2)

            if stage == 30:
                continue
            def t_tr(t):
                for f in range(NFC):
                    ins = t.transpose(pb_b[0][:, f * 128:(f + 1) * 128],
                                      xlnb[:, f * 128:(f + 1) * 128], c["id_b"][:])
                return ins
            ch.grp("tensor", t_tr)
            ch.grp("vector", lambda v: v.tensor_copy(out=xlnT_c[:],
                                                     in_=pb_b[0][:]))

            if stage == 31:
                continue
            def t_qk(t):
                for fo in range(16):
                    bank = pb[1 + fo // 4]
                    reg = bank[:, (fo % 4) * 128:(fo % 4 + 1) * 128]
                    for f in range(NFC):
                        ins = t.matmul(reg,
                                       wqkv_sb[:, f * 3072 + fo * 128:
                                               f * 3072 + (fo + 1) * 128],
                                       xlnT_c[:, f * 128:(f + 1) * 128],
                                       start=(f == 0), stop=(f == NFC - 1))
                return ins
            ch.grp("tensor", t_qk, extra_waits=((wl, w_ready),))

            def v_qk(v):
                for fo in range(16):
                    bank = pb[1 + fo // 4]
                    reg = bank[:, (fo % 4) * 128:(fo % 4 + 1) * 128]
                    ins = v.tensor_scalar(out=qkT_c[:, fo * 128:(fo + 1) * 128],
                                          in0=reg, scalar1=bcolT[:, fo:fo + 1],
                                          scalar2=None, op0=ALU.add)
                return ins
            ch.grp("vector", v_qk)

            if stage == 32:
                continue
            def t_v(t):
                for oc in range(2):
                    for f in range(NFC):
                        t.matmul(pb[5 + oc][:],
                                 xlnT_c[:, f * 128:(f + 1) * 128],
                                 wqkv_sb[:, f * 3072 + VO + oc * 512:
                                         f * 3072 + VO + (oc + 1) * 512],
                                 start=(f == 0), stop=False)
                    ins = t.matmul(pb[5 + oc][:], c["ones_row_b"][:, 0:128],
                                   brow_b[:, VO + oc * 512:VO + (oc + 1) * 512],
                                   start=False, stop=True)
                return ins
            ch.grp("tensor", t_v)

            def v_v(v):
                ins = v.tensor_copy(
                    out=v_c[:].rearrange("p (h s) -> p h s", h=16)[:, 0:8, 0:64],
                    in_=pb[5][:].rearrange("p (h s) -> p h s", h=8))
                ins = v.tensor_copy(
                    out=v_c[:].rearrange("p (h s) -> p h s", h=16)[:, 8:16, 0:64],
                    in_=pb[6][:].rearrange("p (h s) -> p h s", h=8))
                return ins
            ch.grp("vector", v_v)

            if stage == 33:
                continue
            sides = [("k", 8)] + ([("q", 0)] if tc < min(32, ntc2) else [])
            for sname, fo0 in sides:
                def t_xp(t, fo0=fo0):
                    for h in range(L1A_H):
                        bank = pb[h % 8]
                        reg = bank[:, (h // 8) * 256:(h // 8 + 1) * 256]
                        fo = fo0 + h // 2
                        rows = slice((h % 2) * 64, (h % 2) * 64 + 64)
                        ins = t.matmul(reg, qkT_c[rows, fo * 128:(fo + 1) * 128],
                                       projT[rows, h * 256:(h + 1) * 256],
                                       start=True, stop=True)
                    return ins
                ch.grp("tensor", t_xp)

                def v_mx(v):
                    for b in range(8):
                        ins = v.tensor_reduce(
                            out=mx2[:, b * 2:(b + 1) * 2],
                            in_=pb[b][:].rearrange("p (a m) -> p a m", a=2),
                            axis=AXX, op=ALU.max)
                    return ins
                ch.grp("vector", v_mx)
                ch.grp("vector", lambda v: v.tensor_scalar(
                    out=nm[:], in0=mx2[:], scalar1=-L1A_DN, scalar2=None,
                    op0=ALU.mult))

                def s_xp(s):
                    for h in range(L1A_H):
                        bank = pb[h % 8]
                        reg = bank[:, (h // 8) * 256:(h // 8 + 1) * 256]
                        j = 2 * (h % 8) + (h // 8)
                        ins = s.activation(out=xpe[:, h * 256:(h + 1) * 256],
                                           in_=reg, func=AF.Exp,
                                           bias=nm[:, j:j + 1], scale=L1A_DN)
                    return ins
                ch.grp("scalar", s_xp)
                ch.grp("vector", lambda v: v.tensor_scalar(
                    out=kp_c[:], in0=xpe[:], scalar1=L1A_EPS, scalar2=None,
                    op0=ALU.add), extra_waits=((qs, qs.v),))
                if sname == "q":
                    ch.side("sync", lambda q, tc=tc: [q.dma_start(
                        out=qp_dram[tc * 128:(tc + 1) * 128, :], in_=kp_c[:])],
                        ch.v(), qs)
                    if debug and tc == 0:
                        ch.grp("sync", lambda q: q.dma_start(
                            out=dbg_qp, in_=kp_c[:]), dma=True)
                elif stage >= 35 or stage >= 9:
                    def t_kv(t):
                        for h in range(L1A_H):
                            for mc in range(2):
                                i = h * 2 + mc
                                bank = pb[i // 7]
                                reg = bank[:, (i % 7) * 65:(i % 7 + 1) * 65]
                                ins = t.matmul(
                                    reg,
                                    kp_c[:, h * 256 + mc * 128:
                                         h * 256 + (mc + 1) * 128],
                                    v_c[:, h * 65:(h + 1) * 65],
                                    start=True, stop=True)
                        return ins
                    ch.grp("tensor", t_kv)

                    if debug and tc == 0:
                        ch.grp("sync", lambda q: q.dma_start(
                            out=dbg_kp, in_=kp_c[:]), dma=True)
                        ch.grp("sync", lambda q: q.dma_start(
                            out=dbg_vc, in_=v_c[:]), dma=True)
                    def v_kv(v):
                        for b in range(5):
                            n = 455 if b < 4 else 260
                            ins = v.tensor_tensor(
                                out=kv_acc[:, b * 512:b * 512 + n],
                                in0=pb[b][:, 0:n],
                                in1=kv_acc[:, b * 512:b * 512 + n], op=ALU.add)
                        return ins
                    ch.grp("vector", v_kv)

        if stage >= 4:
            ch.grp("vector", lambda v: v.tensor_copy(out=kvb[:], in_=kv_acc[:]))
            if debug:
                ch.grp("sync", lambda q: q.dma_start(out=dbg_kv,
                                                     in_=kv_acc[:]), dma=True)

        buf2_free = [0, 0]
        for tc in range(ntc2 if stage >= 5 else 0):
            bi = tc % 2
            ch.side("sync", lambda q, tc=tc, bi=bi: [q.dma_start(
                out=qp_r[bi][:], in_=qp_dram[tc * 128:(tc + 1) * 128, :])],
                buf2_free[bi], ql, extra_waits=((qs, qs.v),))
            need_ql = ql.v
            ch.side("sync", lambda q, tc=tc, bi=bi: [q.dma_start(
                out=x2r[bi][:], in_=x2[tc * 128:(tc + 1) * 128, :])],
                buf2_free[bi], xl)
            need_xl = xl.v

            def t_q(t, bi=bi):
                for i in range(32):
                    bank = pb_b[4 + i // 8]
                    ins = t.transpose(bank[:, (i % 8) * 128:(i % 8 + 1) * 128],
                                      qp_r[bi][:, i * 128:(i + 1) * 128],
                                      c["id_b"][:])
                return ins
            buf2_free[bi] = ch.grp("tensor", t_q, extra_waits=((ql, need_ql),))

            def v_q(v):
                for b in range(4):
                    ins = v.tensor_copy(out=qpT[:, b * 1024:(b + 1) * 1024],
                                        in_=pb_b[4 + b][:])
                return ins
            ch.grp("vector", v_q)

            def t_n(t):
                for h in range(L1A_H):
                    bank = pb[h // 7]
                    reg = bank[:, (h % 7) * 65:(h % 7 + 1) * 65]
                    for mc in range(2):
                        i = h * 2 + mc
                        ko = (i // 7) * 512 + (i % 7) * 65
                        ins = t.matmul(reg, qpT[:, i * 128:(i + 1) * 128],
                                       kvb[:, ko:ko + 65],
                                       start=(mc == 0), stop=(mc == 1))
                return ins
            ch.grp("tensor", t_n)

            def v_d1(v):
                for h in range(L1A_H):
                    bank = pb[h // 7]
                    ins = v.tensor_scalar(
                        out=dn[:, h:h + 1],
                        in0=bank[:, (h % 7) * 65 + 64:(h % 7) * 65 + 65],
                        scalar1=L1A_EPS, scalar2=None, op0=ALU.add)
                return ins
            ch.grp("vector", v_d1)
            ch.grp("vector", lambda v: v.reciprocal(out=dn[:], in_=dn[:]))

            def v_d2(v):
                for h in range(L1A_H):
                    bank = pb[h // 7]
                    ins = v.tensor_scalar(
                        out=attn_c[:, h * 64:(h + 1) * 64],
                        in0=bank[:, (h % 7) * 65:(h % 7) * 65 + 64],
                        scalar1=dn[:, h:h + 1], scalar2=None, op0=ALU.mult)
                return ins
            ch.grp("vector", v_d2)

            if debug and tc == 0:
                ch.grp("sync", lambda q: q.dma_start(out=dbg_at,
                                                     in_=attn_c[:]), dma=True)
            def t_at(t):
                for f in range(NFC):
                    ins = t.transpose(pb_b[3][:, f * 128:(f + 1) * 128],
                                      attn_c[:, f * 128:(f + 1) * 128],
                                      c["id_b"][:])
                return ins
            ch.grp("tensor", t_at)
            ch.grp("vector", lambda v: v.tensor_copy(out=attnT_c[:],
                                                     in_=pb_b[3][:]))

            def t_o(t):
                for oc in range(2):
                    for f in range(NFC):
                        t.matmul(pb[5 + oc][:],
                                 attnT_c[:, f * 128:(f + 1) * 128],
                                 wout_sb[:, f * 1024 + oc * 512:
                                         f * 1024 + (oc + 1) * 512],
                                 start=(f == 0), stop=False)
                    ins = t.matmul(pb[5 + oc][:], c["ones_row_b"][:, 0:128],
                                   bout_b[:, oc * 512:(oc + 1) * 512],
                                   start=False, stop=True)
                return ins
            ch.grp("tensor", t_o)

            def v_o(v, bi=bi):
                v.tensor_tensor(out=out_c[:, 0:512], in0=pb[5][:],
                                in1=x2r[bi][:, 0:512], op=ALU.add)
                return v.tensor_tensor(out=out_c[:, 512:1024], in0=pb[6][:],
                                       in1=x2r[bi][:, 512:1024], op=ALU.add)
            vo = ch.grp("vector", v_o, extra_waits=((xl, need_xl),))
            buf2_free[bi] = max(buf2_free[bi], vo)

            ch.side("sync", lambda q, tc=tc: [q.dma_start(
                out=out[tc * 128:(tc + 1) * 128, :], in_=out_c[:])],
                ch.v(), st)
            ch.grp("sync", lambda q: q.nop(), extra_waits=((st, st.v),))

        ch.grp("sync", lambda q: q.nop(), extra_waits=((st, st.v), (qs, qs.v)))
    p.close()
    return p


def run_l1attn(x2_full, w_qkv, b_qkv, w_out, b_out, proj):
    p = _get("l1attn", build_l1attn)
    xf = x2_full.reshape(4, 2, TOWN, D)
    in_maps = []
    for cid in range(NCORE):
        b, h = cid // 2, cid % 2
        xrot = np.concatenate([xf[b, h], xf[b, 1 - h]], 0)
        in_maps.append({
            "x2": np.ascontiguousarray(xrot),
            "w_qkv": w_qkv, "b_qkv": b_qkv, "w_out": w_out, "b_out": b_out,
            "proj": proj,
        })
    res = run_bass_kernel_spmd(p.nc, in_maps, core_ids=list(range(NCORE)))
    outs = [res.results[i]["out"] for i in range(NCORE)]
    return np.concatenate(outs, 0).reshape(4, 8192, D)




L0F_H = 4096
L0F_E = 8
L0F_TB = 512
L0F_NTB = TOWN // L0F_TB
L0F_NIC = L0F_H // 128


def build_l0ffn(debug=False):
    p = Prog()
    nc = p.nc
    x1 = p.dram_in("x1", [SEQ, D])
    alpha = p.dram_in("alpha", [1, 1])
    w_gate = p.dram_in("w_gate", [D, L0F_E])
    b_gate = p.dram_in("b_gate", [1, L0F_E])
    W1m = p.dram_in("W1m", [D, L0F_H])
    b1m = p.dram_in("b1m", [1, L0F_H])
    W2m = p.dram_in("W2m", [L0F_H, D])
    b2m = p.dram_in("b2m", [1, D])
    W1e = p.dram_in("W1e", [L0F_E, D, L0F_H])
    b1e = p.dram_in("b1e", [L0F_E, L0F_H])
    W2e = p.dram_in("W2e", [L0F_E, L0F_H, D])
    b2e = p.dram_in("b2e", [L0F_E, D])
    out = p.dram_out("out", [TOWN, D])
    w1sel_dram = p.dram_tmp("w1sel", [2, D, L0F_H], BF16)
    w2sel_dram = p.dram_tmp("w2sel", [2, L0F_H, D], BF16)

    xsb = [p.sbuf(f"xsb{i}", [128, D], F32) for i in range(2)]
    xlnb = p.sbuf("xlnb", [128, D], BF16)
    xT = [p.sbuf(f"xT{i}", [128, TOWN], BF16) for i in range(NFC)]
    gw_sb = p.sbuf("gw_sb", [128, NFC * 8], F32)
    gb_sb = p.sbuf("gb_sb", [1, 8], F32)
    al_sb = p.sbuf("al_sb", [1, 4], F32)
    gsum = p.sbuf("gsum", [128, 8], F32)
    gE = p.sbuf("gE", [1, 8], F32)
    gEm = p.sbuf("gEm", [1, 8], F32)
    gZ = p.sbuf("gZ", [1, 8], F32)
    gm1 = p.sbuf("gm1", [1, 1], F32)
    grs = p.sbuf("grs", [1, 1], F32)
    gmask = p.sbuf("gmask", [1, 8], F32)
    gidx = p.sbuf("gidx", [1, 8], U32)
    gidx_f = p.sbuf("gidx_f", [1, 8], F32)
    cwk = p.sbuf("cwk", [1, 8], F32)
    sc_bc = p.sbuf("sc_bc", [128, 3], F32)
    iot2 = p.sbuf("iot2", [128, 1], F32)
    iot1 = p.sbuf("iot1", [128, 1], F32)
    basek = p.sbuf("basek", [128, 2], F32)
    base2k = p.sbuf("base2k", [128, 2], F32)
    idxcol = p.sbuf("idxcol", [128, 1], U32)
    gstage = p.sbuf("gstage", [128, 2048], BF16)
    brow = [p.sbuf(f"brow{i}", [1, L0F_H], BF16) for i in range(2)]
    b1m_b = p.sbuf("b1m_b", [1, L0F_H], BF16)
    b2m_b = p.sbuf("b2m_b", [1, D], BF16)
    b1s_b = [p.sbuf(f"b1s_b{i}", [1, L0F_H], BF16) for i in range(2)]
    b2s_b = [p.sbuf(f"b2s_b{i}", [1, D], BF16) for i in range(2)]
    w1_r = [p.sbuf(f"w1r{i}", [128, NFC * 128], BF16) for i in range(2)]
    wd_r = [p.sbuf(f"wdr{i}", [128, D], BF16) for i in range(2)]
    hT = [p.sbuf(f"hT{i}", [128, L0F_TB], BF16) for i in range(L0F_NIC)]
    accs = [p.sbuf(f"acc{i}", [128, D], F32) for i in range(L0F_TB // 128)]
    x1r = [p.sbuf(f"x1r{i}", [128, D], F32) for i in range(L0F_TB // 128)]
    ln_scr = alloc_ln(p)

    pall = [p.psum(f"pa{i}") for i in range(8)]
    pg = pall[0:2]
    pt = pall[2]
    pt_b = pt.bitcast(BF16)
    pgate = pall[3]

    ld = p.sem("ld")
    wl = p.sem("wl")
    wdl = p.sem("wdl")
    xl = p.sem("xl")
    st = p.sem("st")
    gl = p.sem("gl")

    with nc.Block() as blk:
        ch = Chain(p, blk)
        c = emit_consts(p, ch)
        c["ones_col_b"] = p.sbuf("c_ones_col_b", [128, 1], BF16)
        ch.grp("gpsimd", lambda g: g.memset(c["ones_col_b"][:], 1.0))
        ch.grp("gpsimd", lambda g: g.iota(
            iot2[:], pattern=[[0, 1]], base=0, channel_multiplier=2,
            allow_small_or_imprecise_dtypes=True))
        ch.grp("gpsimd", lambda g: g.iota(
            iot1[:], pattern=[[0, 1]], base=0, channel_multiplier=1,
            allow_small_or_imprecise_dtypes=True))

        def g_w0(g):
            return [
                g.dma_start(out=gw_sb[:].rearrange("p (f e) -> p f e", f=NFC),
                            in_=w_gate.rearrange("(f p) e -> p f e", p=128)),
                g.dma_start(out=gb_sb[:], in_=b_gate),
                g.dma_start(out=al_sb[:, 0:1], in_=alpha),
            ]
        ch.side("gpsimd", g_w0, 0, gl)
        ch.side("gpsimd", lambda g: [g.dma_start(out=b1m_b[:], in_=b1m),
                                     g.dma_start(out=b2m_b[:], in_=b2m)], 0, gl)
        gl_w = gl.v

        NTC = SEQ // 128
        buf_free = [0, 0]
        for tc in range(NTC):
            bi = tc % 2
            ch.side("sync", lambda q, tc=tc, bi=bi: [q.dma_start(
                out=xsb[bi][:], in_=x1[tc * 128:(tc + 1) * 128, :])],
                buf_free[bi], ld)
            need_ld = ld.v
            ch.grp("vector", lambda v, bi=bi: v.bn_stats(
                out=ln_scr["t12"][:, 0:6], in_=xsb[bi][:, 0:512]),
                extra_waits=((ld, need_ld),))
            ch.grp("vector", lambda v, bi=bi: v.bn_stats(
                out=ln_scr["t12"][:, 6:12], in_=xsb[bi][:, 512:1024]))
            ch.grp("vector", lambda v: v.bn_aggr(out=ln_scr["mv"][:],
                                                 in_=ln_scr["t12"][:]))
            ch.grp("vector", lambda v: v.tensor_scalar(
                out=ln_scr["rs"][:], in0=ln_scr["mv"][:, 1:2], scalar1=EPS_LN,
                scalar2=None, op0=ALU.add))
            ch.grp("scalar", lambda s: s.activation(
                out=ln_scr["rs"][:], in_=ln_scr["rs"][:], func=AF.Ln))
            ch.grp("scalar", lambda s: s.activation(
                out=ln_scr["rs"][:], in_=ln_scr["rs"][:], func=AF.Exp,
                scale=-0.5))

            def v2(v, bi=bi):
                return v.tensor_scalar(out=xlnb[:], in0=xsb[bi][:],
                                       scalar1=ln_scr["mv"][:, 0:1],
                                       scalar2=ln_scr["rs"][:],
                                       op0=ALU.subtract, op1=ALU.mult)
            buf_free[bi] = ch.grp("vector", v2)

            def t_gs(t, tc=tc):
                for fc in range(NFC):
                    ins = t.matmul(pgate[:, fc:fc + 1],
                                   xlnb[:, fc * 128:(fc + 1) * 128],
                                   c["ones_col_b"][:],
                                   start=(tc == 0 and fc == 0),
                                   stop=(tc == NTC - 1 and fc == NFC - 1))
                return ins
            ch.grp("tensor", t_gs)

            if tc < 32:
                def t_tr(t):
                    for f in range(NFC):
                        ins = t.transpose(pt_b[:, f * 128:(f + 1) * 128],
                                          xlnb[:, f * 128:(f + 1) * 128],
                                          c["id_b"][:])
                    return ins
                ch.grp("tensor", t_tr)

                def v_ev(v, tc=tc):
                    for f in range(NFC):
                        ins = v.tensor_copy(out=xT[f][:, tc * 128:(tc + 1) * 128],
                                            in_=pt_b[:, f * 128:(f + 1) * 128])
                    return ins
                ch.grp("vector", v_ev)

        ch.grp("vector", lambda v: v.tensor_scalar(
            out=gsum[:], in0=pgate[:, 0:8], scalar1=1.0 / SEQ, scalar2=None,
            op0=ALU.mult))

        def t_gl(t):
            for fc in range(NFC):
                t.matmul(pgate[0:1, 16:24], gsum[:, fc:fc + 1],
                         gw_sb[:, fc * 8:(fc + 1) * 8],
                         start=(fc == 0), stop=False)
            return t.matmul(pgate[0:1, 16:24], c["ones_row"][0:1, 0:1],
                            gb_sb[:], start=False, stop=True)
        ch.grp("tensor", t_gl, extra_waits=((gl, gl_w),))
        ch.grp("vector", lambda v: v.tensor_reduce(
            out=gm1[:], in_=pgate[0:1, 16:24], axis=AXX, op=ALU.max))
        ch.grp("vector", lambda v: v.tensor_scalar(
            out=gm1[:], in0=gm1[:], scalar1=-1.0, scalar2=None, op0=ALU.mult))
        ch.grp("scalar", lambda s: s.activation(
            out=gE[:], in_=pgate[0:1, 16:24], func=AF.Exp, bias=gm1[:]))
        ch.grp("vector", lambda v: v.max(out=gEm[:], in_=gE[:]))
        ch.grp("vector", lambda v: v.tensor_tensor(
            out=grs[:], in0=gEm[:, 0:1], in1=gEm[:, 1:2], op=ALU.add))
        ch.grp("vector", lambda v: v.reciprocal(out=grs[:], in_=grs[:]))
        ch.grp("vector", lambda v: v.max_index(out=gidx[:], in_max=gEm[:],
                                               in_values=gE[:]))
        ch.grp("vector", lambda v: v.tensor_copy(out=gidx_f[:], in_=gidx[:]))
        ch.grp("vector", lambda v: v.tensor_copy(out=gZ[:], in_=gE[:]))
        ch.grp("vector", lambda v: v.memset(gEm[:, 2:8], -3.0e38))
        ch.grp("vector", lambda v: v.match_replace(
            out=gZ[:], in_to_replace=gEm[:], in_values=gZ[:], imm_value=-1.0))
        ch.grp("vector", lambda v: v.tensor_scalar(
            out=gmask[:], in0=gZ[:], scalar1=-1.0, scalar2=None,
            op0=ALU.is_equal))
        ch.grp("vector", lambda v: v.tensor_tensor(
            out=gmask[:], in0=gmask[:], in1=gE[:], op=ALU.mult))
        ch.grp("vector", lambda v: v.tensor_scalar(
            out=cwk[:], in0=gmask[:], scalar1=grs[:], scalar2=None,
            op0=ALU.mult))
        ch.grp("vector", lambda v: v.tensor_scalar(
            out=al_sb[:, 1:2], in0=al_sb[:, 0:1], scalar1=0.1, scalar2=1.0,
            op0=ALU.max, op1=ALU.min))
        ch.grp("vector", lambda v: v.tensor_scalar(
            out=al_sb[:, 2:3], in0=al_sb[:, 1:2], scalar1=-1.0, scalar2=1.0,
            op0=ALU.mult, op1=ALU.add))
        ch.grp("vector", lambda v: v.tensor_tensor(
            out=gEm[:, 0:2], in0=gEm[:, 0:2],
            in1=grs[:].to_broadcast([1, 2]), op=ALU.mult))
        ch.grp("vector", lambda v: v.tensor_tensor(
            out=gEm[:, 0:2], in0=gEm[:, 0:2],
            in1=al_sb[:, 2:3].to_broadcast([1, 2]), op=ALU.mult))
        def t_bc(t):
            t.matmul(pgate[:, 32:33], c["ones_row"][0:1, 0:128],
                     al_sb[:, 1:2], start=True, stop=True)
            t.matmul(pgate[:, 33:34], c["ones_row"][0:1, 0:128],
                     gEm[:, 0:1], start=True, stop=True)
            return t.matmul(pgate[:, 34:35], c["ones_row"][0:1, 0:128],
                            gEm[:, 1:2], start=True, stop=True)
        ch.grp("tensor", t_bc)
        ch.grp("vector", lambda v: v.tensor_copy(out=sc_bc[:],
                                                 in_=pgate[:, 32:35]))

        def t_eb(t):
            for k in range(2):
                ins = t.matmul(pgate[:, 40 + k:41 + k],
                               c["ones_row"][0:1, 0:128],
                               gidx_f[:, k:k + 1], start=True, stop=True)
            return ins
        ch.grp("tensor", t_eb)

        def v_eb(v):
            v.scalar_tensor_tensor(out=basek[:, 0:1], in0=pgate[:, 40:41],
                                   scalar=2048.0, in1=iot2[:], op0=ALU.mult,
                                   op1=ALU.add)
            v.scalar_tensor_tensor(out=basek[:, 1:2], in0=pgate[:, 41:42],
                                   scalar=2048.0, in1=iot2[:], op0=ALU.mult,
                                   op1=ALU.add)
            v.scalar_tensor_tensor(out=base2k[:, 0:1], in0=pgate[:, 40:41],
                                   scalar=4096.0, in1=iot1[:], op0=ALU.mult,
                                   op1=ALU.add)
            return v.scalar_tensor_tensor(out=base2k[:, 1:2],
                                          in0=pgate[:, 41:42], scalar=4096.0,
                                          in1=iot1[:], op0=ALU.mult,
                                          op1=ALU.add)
        ch.grp("vector", v_eb)

        W1e_v = W1e.rearrange("e d h -> (e d) h").rearrange(
            "r (a c) -> (r a) c", a=2)
        W2e_v = W2e.rearrange("e h d -> (e h) d")
        for k in range(2):
            for g in range(8):
                for half in range(2):
                    ch.grp("vector", lambda v, k=k, g=g, half=half:
                           v.tensor_scalar(out=idxcol[:], in0=basek[:, k:k + 1],
                                           scalar1=float(256 * g + half),
                                           scalar2=None, op0=ALU.add))
                    ch.grp("gpsimd", lambda g_, k=k, g=g, half=half:
                           g_.indirect_dma_start(
                               out=gstage[:], out_offset=None, in_=W1e_v,
                               in_offset=bass.IndirectOffsetOnAxis(
                                   ap=idxcol[:], axis=0)), dma=True)
                    ch.grp("sync", lambda q, k=k, g=g, half=half:
                           q.dma_start(
                               out=w1sel_dram[k][g * 128:(g + 1) * 128,
                                                 half * 2048:(half + 1) * 2048],
                               in_=gstage[:]), dma=True)
            for g in range(32):
                ch.grp("vector", lambda v, k=k, g=g: v.tensor_scalar(
                    out=idxcol[:], in0=base2k[:, k:k + 1],
                    scalar1=float(128 * g), scalar2=None, op0=ALU.add))
                ch.grp("gpsimd", lambda g_, k=k, g=g: g_.indirect_dma_start(
                    out=gstage[:, 0:1024], out_offset=None, in_=W2e_v,
                    in_offset=bass.IndirectOffsetOnAxis(ap=idxcol[:], axis=0)),
                    dma=True)
                ch.grp("sync", lambda q, k=k, g=g: q.dma_start(
                    out=w2sel_dram[k][g * 128:(g + 1) * 128, :],
                    in_=gstage[:, 0:1024]), dma=True)
            for e in range(L0F_E):
                ch.side("gpsimd", lambda g, e=e: [
                    g.dma_start(out=brow[0][:], in_=b1e[e].unsqueeze(0)),
                    g.dma_start(out=brow[1][:, 0:D], in_=b2e[e].unsqueeze(0))],
                    ch.v(), gl)
                need = gl.v
                ch.grp("vector", lambda v, e=e, k=k: v.tensor_scalar(
                    out=grs[:], in0=gidx_f[:, k:k + 1], scalar1=float(e),
                    scalar2=None, op0=ALU.is_equal), extra_waits=((gl, need),))
                if e == 0:
                    ch.grp("vector", lambda v, k=k: v.tensor_scalar(
                        out=b1s_b[k][:], in0=brow[0][:], scalar1=grs[:],
                        scalar2=None, op0=ALU.mult))
                    ch.grp("vector", lambda v, k=k: v.tensor_scalar(
                        out=b2s_b[k][:], in0=brow[1][:, 0:D], scalar1=grs[:],
                        scalar2=None, op0=ALU.mult))
                else:
                    ch.grp("vector", lambda v, k=k: v.scalar_tensor_tensor(
                        out=b1s_b[k][:], in0=brow[0][:], scalar=grs[:],
                        in1=b1s_b[k][:], op0=ALU.mult, op1=ALU.add))
                    ch.grp("vector", lambda v, k=k: v.scalar_tensor_tensor(
                        out=b2s_b[k][:], in0=brow[1][:, 0:D], scalar=grs[:],
                        in1=b2s_b[k][:], op0=ALU.mult, op1=ALU.add))

        NTCH = L0F_TB // 128
        wd_free_v = 0
        x1r_free_v = [0] * NTCH
        for tb in range(L0F_NTB):
            t0 = tb * L0F_TB
            x1_need = []
            for tch in range(NTCH):
                ch.side("sync", lambda q, tch=tch: [q.dma_start(
                    out=x1r[tch][:],
                    in_=x1[t0 + tch * 128:t0 + (tch + 1) * 128, :])],
                    x1r_free_v[tch], xl)
                x1_need.append(xl.v)
            for br in range(3):
                if br == 0:
                    w1_src, wd_src = W1m, W2m
                    b1row, b2row = b1m_b, b2m_b
                else:
                    w1_src, wd_src = w1sel_dram[br - 1], w2sel_dram[br - 1]
                    b1row, b2row = b1s_b[br - 1], b2s_b[br - 1]

                wg_need = [0, 0]
                wg_free_v = [0, 0]
                for ic in range(L0F_NIC):
                    ring = ic % 2

                    def g_w1(g, ic=ic, ring=ring, w1_src=w1_src):
                        return [g.dma_start(
                            out=w1_r[ring][:].rearrange("p (f c) -> p f c", f=NFC),
                            in_=w1_src[:, ic * 128:(ic + 1) * 128]
                                .rearrange("(f p) c -> p f c", p=128))]
                    ch.side("gpsimd", g_w1, wg_free_v[ring], wl)
                    wg_need[ring] = wl.v

                    def t_h(t, ic=ic, ring=ring, tb=tb, b1row=b1row):
                        for f in range(NFC):
                            t.matmul(pg[ring][:],
                                     w1_r[ring][:, f * 128:(f + 1) * 128],
                                     xT[f][:, t0:t0 + 512],
                                     start=(f == 0), stop=False)
                        ins = t.matmul(pg[ring][:],
                                       b1row[:, ic * 128:(ic + 1) * 128],
                                       c["ones_row_b"][:],
                                       start=False, stop=True)
                        return ins
                    wg_free_v[ring] = ch.grp("tensor", t_h,
                                             extra_waits=((wl, wg_need[ring]),))

                    def s_g(s, ic=ic, ring=ring):
                        return s.activation(out=hT[ic][:], in_=pg[ring][:],
                                            func=AF.Gelu_apprx_tanh)
                    ch.grp("scalar", s_g)
                wdr_free = [0, 0]
                wdr_need = [0, 0]
                for ic in range(L0F_NIC):
                    ring = ic % 2

                    def g_wd(g, ic=ic, ring=ring, wd_src=wd_src):
                        return [g.dma_start(
                            out=wd_r[ring][:],
                            in_=wd_src[ic * 128:(ic + 1) * 128, :])]
                    ch.side("gpsimd", g_wd, wdr_free[ring], wdl)
                    wdr_need[ring] = wdl.v

                    def t_y(t, ic=ic, ring=ring, b2row=b2row):
                        for tch in range(NTCH):
                            for oc in range(2):
                                ins = t.matmul(
                                    pall[tch * 2 + oc][:],
                                    hT[ic][:, tch * 128:(tch + 1) * 128],
                                    wd_r[ring][:, oc * 512:(oc + 1) * 512],
                                    start=(ic == 0), stop=False)
                        if ic == L0F_NIC - 1:
                            for tch in range(NTCH):
                                for oc in range(2):
                                    ins = t.matmul(
                                        pall[tch * 2 + oc][:],
                                        c["ones_row_b"][:, 0:128],
                                        b2row[:, oc * 512:(oc + 1) * 512],
                                        start=False, stop=True)
                        return ins
                    wdr_free[ring] = ch.grp("tensor", t_y,
                                            extra_waits=((wdl, wdr_need[ring]),))
                for tch in range(NTCH):
                    for oc in range(2):
                        def v_c(v, tb=tb, br=br, tch=tch, oc=oc):
                            bank = pall[tch * 2 + oc]
                            if br == 0:
                                return v.scalar_tensor_tensor(
                                    out=accs[tch][:, oc * 512:(oc + 1) * 512],
                                    in0=bank[:], scalar=sc_bc[:, 0:1],
                                    in1=x1r[tch][:, oc * 512:(oc + 1) * 512],
                                    op0=ALU.mult, op1=ALU.add)
                            return v.scalar_tensor_tensor(
                                out=accs[tch][:, oc * 512:(oc + 1) * 512],
                                in0=bank[:], scalar=sc_bc[:, br:br + 1],
                                in1=accs[tch][:, oc * 512:(oc + 1) * 512],
                                op0=ALU.mult, op1=ALU.add)
                        if br == 0:
                            ew = ((xl, x1_need[tch]), (st, st.v))
                        else:
                            ew = ()
                        vcv = ch.grp("vector", v_c, extra_waits=ew)
                        if br == 0 and oc == 1:
                            x1r_free_v[tch] = vcv
            for tch in range(NTCH):
                ch.side("sync", lambda q, tch=tch: [q.dma_start(
                    out=out[t0 + tch * 128:t0 + (tch + 1) * 128, :],
                    in_=accs[tch][:])],
                    ch.v(), st)
        ch.grp("sync", lambda q: q.nop(), extra_waits=((st, st.v),))
    p.close()
    return p


def run_l0ffn(x1_full, alpha, w_gate, b_gate, W1m, b1m, W2m, b2m,
              W1e, b1e, W2e, b2e):
    p = _get("l0ffn", build_l0ffn)
    xf = x1_full.reshape(4, 2, TOWN, D)
    in_maps = []
    for cid in range(NCORE):
        b, h = cid // 2, cid % 2
        xrot = np.concatenate([xf[b, h], xf[b, 1 - h]], 0)
        in_maps.append({
            "x1": np.ascontiguousarray(xrot),
            "alpha": np.asarray(alpha, np.float32).reshape(1, 1),
            "w_gate": w_gate, "b_gate": b_gate.reshape(1, -1),
            "W1m": W1m, "b1m": b1m.reshape(1, -1), "W2m": W2m,
            "b2m": b2m.reshape(1, -1), "W1e": W1e, "b1e": b1e, "W2e": W2e,
            "b2e": b2e,
        })
    res = run_bass_kernel_spmd(p.nc, in_maps, core_ids=list(range(NCORE)))
    outs = [res.results[i]["out"] for i in range(NCORE)]
    return np.concatenate(outs, 0).reshape(4, 8192, D)




L0A_H = 16


def build_l0attn(nbb=4, ntc2=64, natt=9999, p2lvl=9, debug=False):
    p = Prog()
    nc = p.nc
    xb = p.dram_in("xb", [SEQ, D])
    x_all = p.dram_in("x_all", [NB * SEQ, D])
    msel = p.dram_in("msel", [1, 2])
    W_Q = p.dram_in("W_Q", [D, D])
    W_K = p.dram_in("W_K", [D, D])
    W_V = p.dram_in("W_V", [D, D])
    W_O = p.dram_in("W_O", [D, D])
    out = p.dram_out("out", [TOWN, D])
    KT_dram = p.dram_tmp("KT_scr", [D, SEQ], BF16)
    V_dram = p.dram_tmp("V_scr", [SEQ, 1040], BF16)
    QT_dram = p.dram_tmp("QT_scr", [D, TOWN], BF16)
    AT_dram = p.dram_tmp("AT_scr", [1040, TOWN], F32)
    if debug:
        dbg_m01 = p.dram_out("dbg_m01", [128, 512], BF16)
        dbg_mex = p.dram_out("dbg_mex", [128, 8192], BF16)
        dbg_at = p.dram_out("dbg_at", [65, 512])
        dbg_kt = p.dram_out("dbg_kt", [128, 256], BF16)
        dbg_qt = p.dram_out("dbg_qt", [128, 512], BF16)
        dbg_vt = p.dram_out("dbg_vt", [128, 1040], BF16)
        dbg_ub = p.dram_out("dbg_ub", [128, 1])

    xsb = [p.sbuf(f"xsb{i}", [128, D], F32) for i in range(2)]
    xlnb = p.sbuf("xlnb", [128, D], BF16)
    xlnT_c = p.sbuf("xlnT_c", [128, D], BF16)
    Amat = p.sbuf("Amat", [128, 4], BF16)
    Atmp = p.sbuf("Atmp", [128, 8], F32)
    m8 = p.sbuf("m8", [128, 8], F32)
    msel_sb = p.sbuf("msel_sb", [1, 2], F32)
    mselbc = p.sbuf("mselbc", [128, 2], F32)
    Sexp = p.sbuf("Sexp", [4, 128], BF16)
    SexpF = p.sbuf("SexpF", [4, 128], F32)
    mexpT = p.sbuf("mexpT", [128, 64 * 128], BF16)
    q2b = p.sbuf("q2b", [128, 128], BF16)
    n2run = p.sbuf("n2run", [128, 2], F32)
    n2one = p.sbuf("n2one", [1, 2], F32)
    ubc = p.sbuf("ubc", [128, 1], F32)
    kstage = p.sbuf("kstage", [128, NFC * 128], BF16)
    vstage = [p.sbuf(f"vstage{i}", [128, 1040], BF16) for i in range(2)]
    ln_scr = alloc_ln(p)
    ar = Arena(p, "l0a_arena", 156000)
    wqf = ar.alloc(128, NFC * D, BF16)
    wkf = ar.alloc(128, NFC * D, BF16)
    wvf = ar.alloc(128, NFC * D, BF16)
    PT_sb = [ar.alloc(128, NFC * 256, BF16) for b in range(NB)]
    QpT = [ar.alloc(128, NFC * 256, BF16) for b in range(NB)]
    KpT = [ar.alloc(128, NFC * 256, BF16) for b in range(NB)]
    asc = [ar.alloc(128, 256, F32) for i in range(2)]
    mz = [ar.alloc(128, 256, F32) for i in range(2)]
    m01 = [ar.alloc(128, 256, BF16) for i in range(2)]
    mown = ar.alloc(128, 256, BF16)
    mownT = ar.alloc(128, 256, BF16)
    mT4 = ar.alloc(4, 64 * 128, BF16)
    ar.reset()
    KT_sb = [ar.alloc(128, TOWN, BF16) for i in range(NFC)]
    V_sb = [ar.alloc(128, 1040, BF16) for i in range(32)]
    QT_r = [ar.alloc(128, NFC * 512, BF16) for i in range(2)]
    pbuf = ar.alloc(128, 2048, BF16)
    astage = ar.alloc(65, 512, F32)
    ar.reset()
    wof = ar.alloc(128, NFC * D, BF16)
    arow = [ar.alloc(65, 128, F32) for i in range(16)]
    attok = ar.alloc(128, 1040, F32)
    attnout = ar.alloc(128, D, BF16)
    attnT = ar.alloc(128, D, BF16)
    x0r = [ar.alloc(128, D, F32) for i in range(2)]
    out_c = ar.alloc(128, D, F32)
    dnr = ar.alloc(128, 16, F32)

    pfull = p.psum("pfull", (128, 4096))
    pf_b = pfull.bitcast(BF16)

    ld = p.sem("ld")
    wl = p.sem("wl")
    kl = p.sem("kl")
    al = p.sem("al")
    st = p.sem("st")

    with nc.Block() as blk:
        ch = Chain(p, blk)
        c = emit_consts(p, ch)
        c["ones_col_b"] = p.sbuf("c_ones_col_b", [128, 1], BF16)
        ch.grp("gpsimd", lambda g: g.memset(c["ones_col_b"][:], 1.0))

        ch.grp("gpsimd", lambda g: g.iota(
            Atmp[:, 0:4], pattern=[[-32, 4]], base=0, channel_multiplier=1,
            allow_small_or_imprecise_dtypes=True))
        ch.grp("vector", lambda v: v.tensor_scalar(
            out=Atmp[:, 4:8], in0=Atmp[:, 0:4], scalar1=0.0, scalar2=None,
            op0=ALU.is_ge))
        ch.grp("vector", lambda v: v.tensor_scalar(
            out=Atmp[:, 0:4], in0=Atmp[:, 0:4], scalar1=32.0, scalar2=None,
            op0=ALU.is_lt))
        ch.grp("vector", lambda v: v.tensor_tensor(
            out=Atmp[:, 0:4], in0=Atmp[:, 0:4], in1=Atmp[:, 4:8], op=ALU.mult))
        ch.grp("vector", lambda v: v.tensor_scalar(
            out=Amat[:], in0=Atmp[:, 0:4], scalar1=1.0 / 32.0, scalar2=None,
            op0=ALU.mult))
        ch.grp("gpsimd", lambda g: g.iota(
            SexpF[:], pattern=[[1, 4], [0, 32]], base=0, channel_multiplier=-1,
            allow_small_or_imprecise_dtypes=True))
        ch.grp("vector", lambda v: v.tensor_scalar(
            out=Sexp[:], in0=SexpF[:], scalar1=0.0, scalar2=None,
            op0=ALU.is_equal))
        ch.grp("gpsimd", lambda g: g.memset(n2run[:], 0.0))
        for i in range(2):
            ch.grp("gpsimd", lambda g, i=i: g.memset(
                vstage[i][:].rearrange("p (h s) -> p h s", h=16)[:, :, 64:65],
                1.0))

        def g_w0(g):
            return [
                g.dma_start(out=wqf[:].rearrange("p (f c) -> p f c", f=NFC),
                            in_=W_Q.rearrange("(f p) c -> p f c", p=128)),
                g.dma_start(out=wkf[:].rearrange("p (f c) -> p f c", f=NFC),
                            in_=W_K.rearrange("(f p) c -> p f c", p=128)),
                g.dma_start(out=wvf[:].rearrange("p (f c) -> p f c", f=NFC),
                            in_=W_V.rearrange("(f p) c -> p f c", p=128)),
                g.dma_start(out=msel_sb[:], in_=msel),
            ]
        ch.side("gpsimd", g_w0, 0, wl)
        w_ready = wl.v

        def emit_ln_chunk(src_ap, bi):
            need_ld = ld.v
            ch.grp("vector", lambda v: v.bn_stats(
                out=ln_scr["t12"][:, 0:6], in_=xsb[bi][:, 0:512]),
                extra_waits=((ld, need_ld),))
            ch.grp("vector", lambda v: v.bn_stats(
                out=ln_scr["t12"][:, 6:12], in_=xsb[bi][:, 512:1024]))
            ch.grp("vector", lambda v: v.bn_aggr(out=ln_scr["mv"][:],
                                                 in_=ln_scr["t12"][:]))
            ch.grp("vector", lambda v: v.tensor_scalar(
                out=ln_scr["rs"][:], in0=ln_scr["mv"][:, 1:2], scalar1=EPS_LN,
                scalar2=None, op0=ALU.add))
            ch.grp("scalar", lambda s: s.activation(
                out=ln_scr["rs"][:], in_=ln_scr["rs"][:], func=AF.Ln))
            ch.grp("scalar", lambda s: s.activation(
                out=ln_scr["rs"][:], in_=ln_scr["rs"][:], func=AF.Exp,
                scale=-0.5))

            def v2(v):
                return v.tensor_scalar(out=xlnb[:], in0=xsb[bi][:],
                                       scalar1=ln_scr["mv"][:, 0:1],
                                       scalar2=ln_scr["rs"][:],
                                       op0=ALU.subtract, op1=ALU.mult)
            return ch.grp("vector", v2)

        buf_free = [0, 0]
        for bb in range(nbb):
            for tc in range(64):
                gi = bb * 64 + tc
                bi = gi % 2
                ch.side("sync", lambda q, gi=gi, bi=bi: [q.dma_start(
                    out=xsb[bi][:], in_=x_all[gi * 128:(gi + 1) * 128, :])],
                    buf_free[bi], ld)
                buf_free[bi] = emit_ln_chunk(None, bi)

                def t_pool(t, tc=tc):
                    for fc in range(NFC):
                        ins = t.matmul(
                            pfull[:, fc * 512 + tc * 4:fc * 512 + tc * 4 + 4],
                            xlnb[:, fc * 128:(fc + 1) * 128], Amat[:],
                            start=True, stop=True)
                    return ins
                ch.grp("tensor", t_pool)
            def v_pev(v, bb=bb):
                for fc in range(NFC):
                    ins = v.tensor_copy(
                        out=PT_sb[bb][:, fc * 256:(fc + 1) * 256],
                        in_=pfull[:, fc * 512:fc * 512 + 256])
                return ins
            ch.grp("vector", v_pev)

        for bb in range(nbb if p2lvl >= 2 else 0):
            for fo in range(NFC):
                def t_q1(t, bb=bb, fo=fo):
                    for f in range(NFC):
                        t.matmul(
                            pfull[:, (fo % 4) * 512:(fo % 4) * 512 + 256],
                            wqf[:, f * D + fo * 128:f * D + (fo + 1) * 128],
                            PT_sb[bb][:, f * 256:(f + 1) * 256],
                            start=(f == 0), stop=(f == NFC - 1))
                    for f in range(NFC):
                        ins = t.matmul(
                            pfull[:, (4 + fo % 4) * 512:(4 + fo % 4) * 512 + 256],
                            wkf[:, f * D + fo * 128:f * D + (fo + 1) * 128],
                            PT_sb[bb][:, f * 256:(f + 1) * 256],
                            start=(f == 0), stop=(f == NFC - 1))
                    return ins
                ch.grp("tensor", t_q1, extra_waits=((wl, w_ready),))
                ch.grp("vector", lambda v, bb=bb, fo=fo: v.tensor_scalar(
                    out=QpT[bb][:, fo * 256:(fo + 1) * 256],
                    in0=pfull[:, (fo % 4) * 512:(fo % 4) * 512 + 256],
                    scalar1=1.0 / 64.0, scalar2=None, op0=ALU.mult))
                ch.grp("vector", lambda v, bb=bb, fo=fo: v.tensor_copy(
                    out=KpT[bb][:, fo * 256:(fo + 1) * 256],
                    in_=pfull[:, (4 + fo % 4) * 512:(4 + fo % 4) * 512 + 256]))
        for bb in range(nbb if p2lvl >= 3 else 0):
            def t_asc(t, bb=bb):
                for h in range(L0A_H):
                    fo = h // 2
                    par = h % 2
                    rows = slice(par * 64, par * 64 + 64)
                    for i2 in range(2):
                        ins = t.matmul(
                            pfull[:, (par * 2 + i2) * 512:
                                  (par * 2 + i2) * 512 + 256],
                            QpT[bb][rows, fo * 256 + i2 * 128:
                                    fo * 256 + i2 * 128 + 128],
                            KpT[bb][rows, fo * 256:(fo + 1) * 256],
                            start=(bb == 0 and h < 2),
                            stop=(bb == nbb - 1 and h >= L0A_H - 2))
                return ins
            ch.grp("tensor", t_asc)
        for i2 in range(2 if p2lvl >= 3 else 0):
            ch.grp("vector", lambda v, i2=i2: v.tensor_copy(
                out=asc[i2][:], in_=pfull[:, i2 * 512:i2 * 512 + 256]))
            ch.grp("vector", lambda v, i2=i2: v.tensor_tensor(
                out=asc[i2][:], in0=pfull[:, (2 + i2) * 512:(2 + i2) * 512 + 256],
                in1=asc[i2][:], op=ALU.add))
            ch.grp("vector", lambda v, i2=i2: v.tensor_copy(
                out=mz[i2][:], in_=asc[i2][:]))
        for i2 in range(2 if p2lvl >= 4 else 0):
            for r in range(7):
                ch.grp("vector", lambda v, i2=i2: v.max(out=m8[:],
                                                        in_=mz[i2][:]))
                if r == 6:
                    ch.grp("vector", lambda v: v.memset(m8[:, 3:8], -3.0e38))
                ch.grp("vector", lambda v, i2=i2: v.match_replace(
                    out=mz[i2][:], in_to_replace=m8[:], in_values=mz[i2][:],
                    imm_value=-3.0e38))
            ch.grp("vector", lambda v, i2=i2: v.tensor_scalar(
                out=m01[i2][:], in0=mz[i2][:], scalar1=-1.0e37, scalar2=None,
                op0=ALU.is_le))
        skip_rest = p2lvl < 5
        def t_msel(t):
            return t.matmul(pfull[:, 3584:3586], c["ones_row"][0:1, 0:128],
                            msel_sb[:], start=True, stop=True)
        if not skip_rest:
            ch.grp("tensor", t_msel)
            ch.grp("vector", lambda v: v.tensor_copy(out=mselbc[:],
                                                     in_=pfull[:, 3584:3586]))
        for b2 in range(2 if not skip_rest else 0):
            ch.grp("vector", lambda v, b2=b2: v.tensor_scalar(
                out=mown[:, b2 * 128:(b2 + 1) * 128],
                in0=m01[0][:, b2 * 128:(b2 + 1) * 128],
                scalar1=mselbc[:, 0:1], scalar2=None, op0=ALU.mult))
            ch.grp("vector", lambda v, b2=b2: v.scalar_tensor_tensor(
                out=mown[:, b2 * 128:(b2 + 1) * 128],
                in0=m01[1][:, (1 - b2) * 128:(2 - b2) * 128],
                scalar=mselbc[:, 1:2],
                in1=mown[:, b2 * 128:(b2 + 1) * 128],
                op0=ALU.mult, op1=ALU.add))
        def t_mt(t):
            t.transpose(pf_b[:, 0:128], mown[:, 0:128], c["id_b"][:])
            return t.transpose(pf_b[:, 128:256], mown[:, 128:256],
                               c["id_b"][:])
        if not skip_rest:
            ch.grp("tensor", t_mt)
            ch.grp("vector", lambda v: v.tensor_copy(out=mownT[:],
                                                     in_=pf_b[:, 0:256]))

        def g_mt4(g):
            ins = []
            for kc in range(64):
                j0 = kc * 4
                half = j0 // 128
                jr = j0 % 128
                ins.append(g.dma_start(
                    out=mT4[:, kc * 128:(kc + 1) * 128],
                    in_=mownT[jr:jr + 4, half * 128:half * 128 + 128]))
            return ins
        if not skip_rest:
            ch.side("gpsimd", g_mt4, ch.v(), kl)
        mt4_need = kl.v
        for ck in range(4 if not skip_rest else 0):
            def t_me(t, ck=ck):
                for k2 in range(16):
                    kc = ck * 16 + k2
                    ins = t.matmul(
                        pfull[:, (k2 // 4) * 512 + (k2 % 4) * 128:
                              (k2 // 4) * 512 + (k2 % 4) * 128 + 128],
                        Sexp[:], mT4[:, kc * 128:(kc + 1) * 128],
                        start=True, stop=True)
                return ins
            ch.grp("tensor", t_me, extra_waits=((kl, mt4_need),))
            ch.grp("vector", lambda v, ck=ck: v.tensor_copy(
                out=mexpT[:, ck * 2048:(ck + 1) * 2048],
                in_=pfull[:, 0:2048]))

        if debug:
            def g_dm(g):
                return [g.dma_start(out=dbg_m01[:, 0:256], in_=m01[0][:]),
                        g.dma_start(out=dbg_m01[:, 256:512], in_=m01[1][:]),
                        g.dma_start(out=dbg_mex, in_=mexpT[:])]
            ch.side("sync", g_dm, ch.v(), st)

        kst_free = 0
        vst_free = [0, 0]
        for tc in range(ntc2):
            bi = tc % 2
            ch.side("sync", lambda q, tc=tc, bi=bi: [q.dma_start(
                out=xsb[bi][:], in_=xb[tc * 128:(tc + 1) * 128, :])],
                buf_free[bi], ld)
            buf_free[bi] = emit_ln_chunk(None, bi)

            def t_tr(t):
                for f in range(NFC):
                    ins = t.transpose(pf_b[:, 6144 + f * 128:6272 + f * 128],
                                      xlnb[:, f * 128:(f + 1) * 128],
                                      c["id_b"][:])
                return ins
            ch.grp("tensor", t_tr)
            ch.grp("vector", lambda v: v.tensor_copy(
                out=xlnT_c[:], in_=pf_b[:, 6144:7168]))

            def t_k(t):
                for fo in range(NFC):
                    bk = fo % 2
                    for f in range(NFC):
                        ins = t.matmul(
                            pfull[:, bk * 512 + (fo // 2) * 128:
                                  bk * 512 + (fo // 2) * 128 + 128],
                            wkf[:, f * D + fo * 128:f * D + (fo + 1) * 128],
                            xlnT_c[:, f * 128:(f + 1) * 128],
                            start=(f == 0), stop=(f == NFC - 1))
                return ins
            ch.grp("tensor", t_k)

            def v_ke(v):
                for fo in range(NFC):
                    bk = fo % 2
                    ins = v.tensor_copy(
                        out=kstage[:, fo * 128:(fo + 1) * 128],
                        in_=pfull[:, bk * 512 + (fo // 2) * 128:
                                  bk * 512 + (fo // 2) * 128 + 128])
                return ins
            ch.grp("vector", v_ke, extra_waits=((kl, kst_free),))
            ch.side("gpsimd", lambda g, tc=tc: [g.dma_start(
                out=KT_dram.rearrange("(f p) k -> p f k", p=128)
                [:, :, tc * 128:(tc + 1) * 128],
                in_=kstage[:].rearrange("p (f c) -> p f c", f=NFC))],
                ch.v(), kl)
            kst_free = kl.v
            for fo in range(NFC):
                ch.grp("vector", lambda v, fo=fo: v.tensor_tensor(
                    out=q2b[:], in0=kstage[:, fo * 128:(fo + 1) * 128],
                    in1=kstage[:, fo * 128:(fo + 1) * 128], op=ALU.mult))
                ch.grp("tensor", lambda t, fo=fo: t.matmul(
                    pfull[:, 1088:1089], q2b[:], c["ones_col_b"][:],
                    start=(fo == 0), stop=(fo == NFC - 1)))
            ch.grp("vector", lambda v: v.tensor_tensor(
                out=n2run[:, 1:2], in0=n2run[:, 1:2], in1=pfull[:, 1088:1089],
                op=ALU.max))

            if tc < 32:
                def t_q(t):
                    for fo in range(NFC):
                        bk = 6 + fo % 2
                        for f in range(NFC):
                            ins = t.matmul(
                                pfull[:, bk * 512 + (fo // 2) * 128:
                                      bk * 512 + (fo // 2) * 128 + 128],
                                wqf[:, f * D + fo * 128:f * D + (fo + 1) * 128],
                                xlnT_c[:, f * 128:(f + 1) * 128],
                                start=(f == 0), stop=(f == NFC - 1))
                    return ins
                ch.grp("tensor", t_q)

                def v_qe(v):
                    for fo in range(NFC):
                        bk = 6 + fo % 2
                        ins = v.tensor_copy(
                            out=kstage[:, fo * 128:(fo + 1) * 128],
                            in_=pfull[:, bk * 512 + (fo // 2) * 128:
                                      bk * 512 + (fo // 2) * 128 + 128])
                    return ins
                ch.grp("vector", v_qe, extra_waits=((kl, kst_free),))
                ch.side("gpsimd", lambda g, tc=tc: [g.dma_start(
                    out=QT_dram.rearrange("(f p) k -> p f k", p=128)
                    [:, :, tc * 128:(tc + 1) * 128],
                    in_=kstage[:].rearrange("p (f c) -> p f c", f=NFC))],
                    ch.v(), kl)
                kst_free = kl.v
                for fo in range(NFC):
                    ch.grp("vector", lambda v, fo=fo: v.tensor_tensor(
                        out=q2b[:], in0=kstage[:, fo * 128:(fo + 1) * 128],
                        in1=kstage[:, fo * 128:(fo + 1) * 128], op=ALU.mult))
                    ch.grp("tensor", lambda t, fo=fo: t.matmul(
                        pfull[:, 1536:1537], q2b[:], c["ones_col_b"][:],
                        start=(fo == 0), stop=(fo == NFC - 1)))
                ch.grp("vector", lambda v: v.tensor_tensor(
                    out=n2run[:, 0:1], in0=n2run[:, 0:1],
                    in1=pfull[:, 1536:1537], op=ALU.max))

            def t_v(t):
                for oc in range(2):
                    for f in range(NFC):
                        ins = t.matmul(
                            pfull[:, (4 + oc) * 512:(4 + oc) * 512 + 512],
                            xlnT_c[:, f * 128:(f + 1) * 128],
                            wvf[:, f * D + oc * 512:f * D + (oc + 1) * 512],
                            start=(f == 0), stop=(f == NFC - 1))
                return ins
            ch.grp("tensor", t_v)
            vri = tc % 2

            def v_v(v, vri=vri):
                v.tensor_copy(
                    out=vstage[vri][:].rearrange("p (h s) -> p h s", h=16)
                    [:, 0:8, 0:64],
                    in_=pfull[:, 2048:2560].rearrange("p (h s) -> p h s", h=8))
                return v.tensor_copy(
                    out=vstage[vri][:].rearrange("p (h s) -> p h s", h=16)
                    [:, 8:16, 0:64],
                    in_=pfull[:, 2560:3072].rearrange("p (h s) -> p h s", h=8))
            ch.grp("vector", v_v, extra_waits=((kl, vst_free[vri]),))
            ch.side("gpsimd", lambda g, tc=tc, vri=vri: [g.dma_start(
                out=V_dram[tc * 128:(tc + 1) * 128, :], in_=vstage[vri][:])],
                ch.v(), kl)
            vst_free[vri] = kl.v

        def t_n2(t):
            t.transpose(pfull[0:1, 1664:1792], n2run[:, 0:1], c["id_f"][:])
            return t.transpose(pfull[0:1, 1792:1920], n2run[:, 1:2],
                               c["id_f"][:])
        ch.grp("tensor", t_n2)
        ch.grp("vector", lambda v: v.tensor_reduce(
            out=n2one[:, 0:1], in_=pfull[0:1, 1664:1792], axis=AXX, op=ALU.max))
        ch.grp("vector", lambda v: v.tensor_reduce(
            out=n2one[:, 1:2], in_=pfull[0:1, 1792:1920], axis=AXX, op=ALU.max))
        ch.grp("scalar", lambda s: s.activation(out=n2one[:], in_=n2one[:],
                                                func=AF.Ln))
        ch.grp("vector", lambda v: v.tensor_tensor(
            out=n2one[:, 0:1], in0=n2one[:, 0:1], in1=n2one[:, 1:2],
            op=ALU.add))
        ch.grp("scalar", lambda s: s.activation(out=n2one[:, 0:1],
                                                in_=n2one[:, 0:1],
                                                func=AF.Exp, scale=0.5))
        ch.grp("vector", lambda v: v.tensor_scalar(
            out=n2one[:, 0:1], in0=n2one[:, 0:1], scalar1=-0.125,
            scalar2=None, op0=ALU.mult))

        def t_ub(t):
            return t.matmul(pfull[:, 1600:1601], c["ones_row"][0:1, 0:128],
                            n2one[:, 0:1], start=True, stop=True)
        ch.grp("tensor", t_ub)
        ch.grp("vector", lambda v: v.tensor_copy(out=ubc[:],
                                                 in_=pfull[:, 1600:1601]))

        if debug:
            def g_dk(g):
                return [g.dma_start(out=dbg_ub, in_=ubc[:])]
            ch.side("sync", g_dk, ch.v(), st)

        natt_done = 0
        qt_free = [0, 0]
        for half in range(2):
            ch.side("gpsimd", lambda g, half=half: [g.dma_start(
                out=KT_sb[f][:],
                in_=KT_dram[f * 128:(f + 1) * 128,
                            half * TOWN:(half + 1) * TOWN])
                for f in range(NFC)], ch.v(), kl)
            ch.side("gpsimd", lambda g, half=half: [g.dma_start(
                out=V_sb[kc][:],
                in_=V_dram[half * TOWN + kc * 128:
                           half * TOWN + (kc + 1) * 128, :])
                for kc in range(32)], ch.v(), kl)
            kv_need = kl.v
            if debug and half == 0:
                def g_dkv(g):
                    return [g.dma_start(out=dbg_kt, in_=KT_sb[0][:, 0:256]),
                            g.dma_start(out=dbg_vt, in_=V_sb[0][:])]
                ch.side("sync", g_dkv, ch.v(), st, extra_waits=((kl, kv_need),))
            for qt in range(8):
                qri = qt % 2
                ch.side("sync", lambda q, qt=qt, qri=qri: [q.dma_start(
                    out=QT_r[qri][:].rearrange("p (f c) -> p f c", f=NFC),
                    in_=QT_dram.rearrange("(f p) k -> p f k", p=128)
                    [:, :, qt * 512:(qt + 1) * 512])],
                    qt_free[qri], ld)
                q_need = ld.v
                for h in range(L0A_H):
                    if natt_done >= natt:
                        break
                    natt_done += 1
                    fo = h // 2
                    r0 = (h % 2) * 64
                    for kcq in range(8):
                        def t_qk(t, kcq=kcq, fo=fo, r0=r0, qri=qri):
                            for k4 in range(4):
                                kc = kcq * 4 + k4
                                ins = t.matmul(
                                    pfull[:, k4 * 512:(k4 + 1) * 512],
                                    KT_sb[fo][r0:r0 + 64,
                                              kc * 128:(kc + 1) * 128],
                                    QT_r[qri][r0:r0 + 64,
                                              fo * 512:(fo + 1) * 512],
                                    start=True, stop=True)
                            return ins
                        ew = ((kl, kv_need), (ld, q_need)) if kcq == 0 else ()
                        ch.grp("tensor", t_qk, extra_waits=ew)
                        ch.grp("scalar", lambda s: s.activation(
                            out=pbuf[:], in_=pfull[:, 0:2048], func=AF.Exp,
                            bias=ubc[:], scale=0.125))

                        def v_msk(v, half=half, qt=qt, kcq=kcq):
                            kbase = half * 32 + kcq * 4
                            in1 = mexpT[:].rearrange(
                                "p (kc t) -> p kc t", kc=64)[
                                :, kbase:kbase + 4, qt * 16:(qt + 1) * 16]
                            in1 = in1.unsqueeze(3).to_broadcast(
                                [128, 4, 16, 32])
                            return v.tensor_tensor(
                                out=pbuf[:].rearrange(
                                    "p (a b c2) -> p a b c2", a=4, b=16),
                                in0=pbuf[:].rearrange(
                                    "p (a b c2) -> p a b c2", a=4, b=16),
                                in1=in1, op=ALU.mult)
                        ch.grp("vector", v_msk)

                        def t_pv(t, h=h, kcq=kcq):
                            for k4 in range(4):
                                kc = kcq * 4 + k4
                                ins = t.matmul(
                                    pfull[0:65, 2560:3072],
                                    V_sb[kc][:, h * 65:(h + 1) * 65],
                                    pbuf[:, k4 * 512:(k4 + 1) * 512],
                                    start=(kcq == 0 and k4 == 0),
                                    stop=(kcq == 7 and k4 == 3))
                            return ins
                        ch.grp("tensor", t_pv)
                    ch.grp("vector", lambda v: v.tensor_copy(
                        out=astage[:], in_=pfull[0:65, 2560:3072]))
                    if half == 0:
                        ch.grp("sync", lambda q, qt=qt, h=h: q.dma_start(
                            out=AT_dram[h * 65:(h + 1) * 65,
                                        qt * 512:(qt + 1) * 512],
                            in_=astage[:]), dma=True)
                    else:
                        ch.grp("gpsimd", lambda g, qt=qt, h=h: g.dma_start(
                            out=AT_dram[h * 65:(h + 1) * 65,
                                        qt * 512:(qt + 1) * 512],
                            in_=astage[:], accum_op=ALU.add), dma=True)
                qt_free[qri] = ch.v()
                if debug and half == 0 and qt == 0:
                    ch.grp("sync", lambda q: q.dma_start(
                        out=dbg_qt, in_=QT_r[0][:, 0:512]), dma=True)

        if debug:
            ch.grp("sync", lambda q: q.dma_start(
                out=dbg_at, in_=AT_dram[0:65, 0:512]), dma=True)

        p4_end = ch.v()
        ch.side("gpsimd", lambda g: [g.dma_start(
            out=wof[:].rearrange("p (f c) -> p f c", f=NFC),
            in_=W_O.rearrange("(f p) c -> p f c", p=128))], p4_end, wl)
        wo_ready = wl.v
        ar_free = p4_end
        x0_free = [p4_end, p4_end]
        for tc in range(32 if natt >= 9999 else 0):
            bi = tc % 2
            ch.side("gpsimd", lambda g, tc=tc: [g.dma_start(
                out=arow[h][:],
                in_=AT_dram[h * 65:(h + 1) * 65, tc * 128:(tc + 1) * 128])
                for h in range(16)], ar_free, al)
            a_need = al.v
            ch.side("sync", lambda q, tc=tc, bi=bi: [q.dma_start(
                out=x0r[bi][:], in_=xb[tc * 128:(tc + 1) * 128, :])],
                x0_free[bi], ld)
            x_need = ld.v

            def t_at(t):
                for h in range(16):
                    ins = t.transpose(
                        pfull[:, (h % 8) * 512 + (h // 8) * 72:
                              (h % 8) * 512 + (h // 8) * 72 + 65],
                        arow[h][:], c["id_f"][0:65, 0:65])
                return ins
            ar_free = ch.grp("tensor", t_at, extra_waits=((al, a_need),))

            def v_at(v):
                for h in range(16):
                    ins = v.tensor_copy(
                        out=attok[:, h * 65:(h + 1) * 65],
                        in_=pfull[:, (h % 8) * 512 + (h // 8) * 72:
                                  (h % 8) * 512 + (h // 8) * 72 + 65])
                return ins
            ch.grp("vector", v_at)

            def v_dn(v):
                for h in range(16):
                    ins = v.tensor_copy(out=dnr[:, h:h + 1],
                                        in_=attok[:, h * 65 + 64:h * 65 + 65])
                return ins
            ch.grp("vector", v_dn)
            ch.grp("vector", lambda v: v.reciprocal(out=dnr[:], in_=dnr[:]))

            def v_no(v):
                for h in range(16):
                    ins = v.tensor_scalar(
                        out=attnout[:, h * 64:(h + 1) * 64],
                        in0=attok[:, h * 65:h * 65 + 64],
                        scalar1=dnr[:, h:h + 1], scalar2=None, op0=ALU.mult)
                return ins
            ch.grp("vector", v_no)

            def t_tr2(t):
                for f in range(NFC):
                    ins = t.transpose(pf_b[:, 6144 + f * 128:6272 + f * 128],
                                      attnout[:, f * 128:(f + 1) * 128],
                                      c["id_b"][:])
                return ins
            ch.grp("tensor", t_tr2)
            ch.grp("vector", lambda v: v.tensor_copy(out=attnT[:],
                                                     in_=pf_b[:, 6144:7168]))

            def t_o(t):
                for oc in range(2):
                    for f in range(NFC):
                        ins = t.matmul(
                            pfull[:, oc * 512:(oc + 1) * 512],
                            attnT[:, f * 128:(f + 1) * 128],
                            wof[:, f * D + oc * 512:f * D + (oc + 1) * 512],
                            start=(f == 0), stop=(f == NFC - 1))
                return ins
            ch.grp("tensor", t_o, extra_waits=((wl, wo_ready),))

            def v_o(v, bi=bi):
                v.tensor_tensor(out=out_c[:, 0:512], in0=pfull[:, 0:512],
                                in1=x0r[bi][:, 0:512], op=ALU.add)
                return v.tensor_tensor(out=out_c[:, 512:1024],
                                       in0=pfull[:, 512:1024],
                                       in1=x0r[bi][:, 512:1024], op=ALU.add)
            x0_free[bi] = ch.grp("vector", v_o, extra_waits=((ld, x_need),))
            ch.side("sync", lambda q, tc=tc: [q.dma_start(
                out=out[tc * 128:(tc + 1) * 128, :], in_=out_c[:])],
                ch.v(), st)
            ch.grp("sync", lambda q: q.nop(), extra_waits=((st, st.v),))

        ch.grp("sync", lambda q: q.nop(), extra_waits=((st, st.v), (kl, kl.v)))
    p.close()
    return p


def run_l0attn(x_full, W_Q, W_K, W_V, W_O):
    p = _get("l0attn", build_l0attn)
    xf = x_full.reshape(4, 2, TOWN, D)
    in_maps = []
    for cid in range(NCORE):
        b, h = cid // 2, cid % 2
        xrot = np.concatenate([xf[b, h], xf[b, 1 - h]], 0)
        ms = np.zeros((1, 2), np.float32)
        ms[0, h] = 1.0
        in_maps.append({
            "xb": np.ascontiguousarray(xrot),
            "x_all": np.ascontiguousarray(x_full.reshape(NB * SEQ, D)),
            "msel": ms,
            "W_Q": W_Q, "W_K": W_K, "W_V": W_V, "W_O": W_O,
        })
    res = run_bass_kernel_spmd(p.nc, in_maps, core_ids=list(range(NCORE)))
    outs = [res.results[i]["out"] for i in range(NCORE)]
    return np.concatenate(outs, 0).reshape(4, 8192, D)


_cache = {}


def _get(name, builder):
    if name not in _cache:
        _cache[name] = builder()
    return _cache[name]


def run_l1ffn(x3_full, gate_w, base_wgu, base_wd, exp_wgu, exp_wd):
    p = _get("l1ffn", build_l1ffn)
    xf = np.ascontiguousarray(x3_full.reshape(NCORE, TOWN, D))
    in_maps = []
    for cid in range(NCORE):
        in_maps.append({
            "x3": xf[cid],
            "gate_w": np.ascontiguousarray(gate_w),
            "base_wgu": np.ascontiguousarray(base_wgu),
            "base_wd": np.ascontiguousarray(base_wd),
            "exp_wgu": np.ascontiguousarray(exp_wgu),
            "exp_wd": np.ascontiguousarray(exp_wd),
        })
    res = run_bass_kernel_spmd(p.nc, in_maps, core_ids=list(range(NCORE)))
    outs = [res.results[i]["out"] for i in range(NCORE)]
    return np.concatenate(outs, 0).reshape(4, 8192, D)


def kernel(**inputs):
    gi = {k: np.ascontiguousarray(np.asarray(v, np.float32))
          for k, v in inputs.items()}
    x = gi["L0_attn_x"]
    x1 = run_l0attn(x, gi["L0_attn_W_Q"], gi["L0_attn_W_K"],
                    gi["L0_attn_W_V"], gi["L0_attn_W_O"])
    x2 = run_l0ffn(x1, gi["L0_ffn_alpha"], gi["L0_ffn_w_gate"],
                   gi["L0_ffn_b_gate"], gi["L0_ffn_W1m"], gi["L0_ffn_b1m"],
                   gi["L0_ffn_W2m"], gi["L0_ffn_b2m"], gi["L0_ffn_W1e"],
                   gi["L0_ffn_b1e"], gi["L0_ffn_W2e"], gi["L0_ffn_b2e"])
    x3 = run_l1attn(x2, gi["L1_attn_w_qkv"], gi["L1_attn_b_qkv"],
                    gi["L1_attn_w_out"], gi["L1_attn_b_out"],
                    gi["L1_attn_proj_matrix"])
    out = run_l1ffn(x3, gi["L1_ffn_gate_w"], gi["L1_ffn_base_wgu"],
                    gi["L1_ffn_base_wd"], gi["L1_ffn_exp_wgu"],
                    gi["L1_ffn_exp_wd"])
    return out.astype(np.float32)
